# version 18
# baseline (speedup 1.0000x reference)
import contextlib
import numpy as np
import ml_dtypes
import concourse.bass as bass
import concourse.mybir as mybir
from concourse.bass_utils import run_bass_kernel_spmd

F32 = mybir.dt.float32
BF16 = mybir.dt.bfloat16
FP8 = mybir.dt.float8e4
AF = mybir.ActivationFunctionType
ALU = mybir.AluOpType
AX = mybir.AxisListType.X

B, S, D = 4, 4096, 2048
H, DH, HH = 16, 128, 8
NT = 16
NSW = 32
HID = 8192
SCALE = float(1.0 / np.sqrt(DH))
NEG = -1e30
DBG_TILE = 0


class Prog:

    ENGS = ("tensor", "vector", "scalar", "sync", "gpsimd")

    def __init__(self, nc, es):
        self.nc = nc
        self.qs = {e: [] for e in self.ENGS}
        self.sems = {}
        self.cnt = {}
        self.waited = {e: {} for e in self.ENGS}
        for name in ("TE", "DV", "AC", "ST", "SX0", "SX1", "SR0", "SR1", "GM",
                     "GW", "GK", "GC", "GS", "S3A", "S3B", "SSP", "SA5",
                     "GA0", "GA1", "WL0", "WL1"):
            self.sems[name] = es.enter_context(nc.semaphore(name))
            self.cnt[name] = 0

    def st(self, q, meth, *args, ctr=None, amt=1, waits=(), **kwargs):
        tok = None
        if ctr is not None:
            self.cnt[ctr] += amt
            tok = self.cnt[ctr]
        wt = []
        for cv in waits:
            if cv is None:
                continue
            c, v = cv
            if v is None or v <= 0:
                continue
            if self.waited[q].get(c, 0) >= v:
                continue
            assert v <= self.cnt[c], f"wait on future {c}={v} (cnt {self.cnt[c]}) in {q}"
            self.waited[q][c] = v
            wt.append((c, v))
        self.qs[q].append((tuple(wt), meth, args, kwargs, ctr, amt))
        return tok

    def run(self, block):
        sems = self.sems

        def runner(eng, ops):
            for wt, meth, args, kwargs, ctr, amt in ops:
                for c, v in wt:
                    eng.wait_ge(sems[c], v)
                if meth is None:
                    continue
                ins = getattr(eng, meth)(*args, **kwargs)
                if ctr is not None:
                    ins.then_inc(sems[ctr], amt)
        for name in self.ENGS:
            getattr(block, name)(lambda eng, ops=tuple(self.qs[name]): runner(eng, ops))


def build_program(debug=()):
    nc = bass.Bass("TRN2", target_bir_lowering=False)
    es = contextlib.ExitStack()

    d_xp = nc.dram_tensor("xp", [2048, D], F32, kind="ExternalInput").ap()
    d_xo = nc.dram_tensor("xo", [2048, D], F32, kind="ExternalInput").ap()
    d_wqT = nc.dram_tensor("wqT", [D, D], F32, kind="ExternalInput").ap()
    d_wkT = nc.dram_tensor("wkT", [D, D], F32, kind="ExternalInput").ap()
    d_wvT = nc.dram_tensor("wvT", [D, D], F32, kind="ExternalInput").ap()
    d_woT = nc.dram_tensor("woT", [D, D], F32, kind="ExternalInput").ap()
    d_wg0 = nc.dram_tensor("wg0t", [64, 128, 16, 128], F32, kind="ExternalInput").ap()
    d_wg1 = nc.dram_tensor("wg1t", [64, 128, 16, 128], F32, kind="ExternalInput").ap()
    d_wl = nc.dram_tensor("wl", [HID, D], F32, kind="ExternalInput").ap()
    d_rc = nc.dram_tensor("ropec", [4096, 512], F32, kind="ExternalInput").ap()
    d_rs = nc.dram_tensor("ropes", [4096, 512], F32, kind="ExternalInput").ap()
    d_msk = nc.dram_tensor("masks", [NT, 128, 640], F32, kind="ExternalInput").ap()
    d_p8 = nc.dram_tensor("p8c", [128, 8], F32, kind="ExternalInput").ap()
    d_id = nc.dram_tensor("identc", [128, 128], F32, kind="ExternalInput").ap()
    d_y = nc.dram_tensor("y", [2048, D], F32, kind="ExternalOutput").ap()
    d_oscr = nc.dram_tensor("o_scr", [NT, 16, 128, 128], BF16, kind="Internal").ap()
    d_xascr = nc.dram_tensor("xa_scr", [2048, D], F32, kind="Internal").ap()
    dbg = {}
    for nm, shp, dt in debug:
        dbg[nm] = nc.dram_tensor(nm, shp, dt, kind="ExternalOutput").ap()

    BASE = (nc._sbuf_addr_for_side("left") + 31) // 32 * 32
    LIMIT = 229376 - 64
    off = [BASE]

    def alloc(name, shape, dtype, at=None):
        if at is None:
            at = off[0]
            sz = int(np.prod(shape[1:])) * mybir.dt.size(dtype)
            off[0] = (at + sz + 31) // 32 * 32
        end = at + int(np.prod(shape[1:])) * mybir.dt.size(dtype)
        assert end <= LIMIT, (name, end, LIMIT)
        return nc.alloc_sbuf_tensor_at(name, shape, dtype, offset=at)

    ck_st = alloc("ck_st", [128, 16, 256], BF16)
    cv_st = alloc("cv_st", [128, 2, 2048], BF16)
    p8sb = alloc("p8sb", [128, 8], BF16)
    idsb = alloc("idsb", [128, 128], BF16)
    ssum = alloc("ssum", [128, 1], F32)
    ssq = alloc("ssq", [128, 1], F32)
    smean = alloc("smean", [128, 1], F32)
    svar = alloc("svar", [128, 1], F32)
    srstd = alloc("srstd", [128, 1], F32)
    snmax = alloc("snmax", [128, 3], F32)
    ssums = alloc("ssums", [128, 3], F32)
    srsum = alloc("srsum", [128, 3], F32)
    snmr = alloc("snmr", [128, 1], F32)
    P_END = off[0]

    SW0 = P_END
    off[0] = SW0
    xbuf = [alloc(f"xbuf{p}", [128, 2048], F32) for p in range(2)]
    sqscr = alloc("sqscr", [128, 2048], F32)
    xnb = [alloc(f"xn{p}", [128, 2048], BF16) for p in range(2)]
    xntb = alloc("xnt", [128, 16, 128], BF16)
    rcb = [alloc(f"rc{p}", [128, 512], F32) for p in range(2)]
    rsb = [alloc(f"rs{p}", [128, 512], F32) for p in range(2)]
    qtok = [alloc(f"qtok{p}", [128, 1024], BF16) for p in range(2)]
    ktok = [alloc(f"ktok{p}", [128, 1024], BF16) for p in range(2)]
    rs1 = alloc("rs1", [128, 512], BF16)
    rs2 = alloc("rs2", [128, 512], BF16)
    rs3 = alloc("rs3", [128, 512], BF16)
    qft = [alloc(f"qft{p}", [128, 8, 128], BF16) for p in range(2)]
    mskb = [alloc(f"mskb{p}", [128, 640], BF16) for p in range(2)]
    pbuf = [alloc(f"pbuf{p}", [128, 640], F32) for p in range(3)]
    pexp = [alloc(f"pexp{p}", [128, 640], BF16) for p in range(3)]
    ptb = [alloc(f"ptb{p}", [128, 5, 128], BF16) for p in range(2)]
    otile = alloc("otile", [128, 8, 128], BF16)
    cvstage = alloc("cvstage", [8, 1024], BF16)
    kring = alloc("kring", [128, 4, 8, 128], BF16)
    vring = alloc("vring", [128, 4, 1024], BF16)
    wqs = alloc("wqs", [128, 16, 1024], BF16)
    wks = alloc("wks", [128, 16, 1024], BF16)
    wvs = alloc("wvs", [128, 16, 1024], BF16)

    off[0] = SW0
    xn2t = alloc("xn2t", [128, 16, 2048], FP8)
    P3W0 = off[0]
    wos = alloc("wos", [128, 16, 2048], BF16)
    o3b = [alloc(f"o3b{p}", [128, 16, 128], BF16) for p in range(2)]
    x3b = [alloc(f"x3b{p}", [128, 2048], F32) for p in range(2)]
    xab = [alloc(f"xab{p}", [128, 2048], F32) for p in range(2)]
    xn2b = [alloc(f"xn2b{p}", [128, 2048], BF16) for p in range(2)]
    sq3 = alloc("sq3", [128, 2048], F32)

    off[0] = P3W0
    act = alloc("act", [128, 64, 2048], FP8)
    P4W0 = off[0]
    wg0b = [alloc(f"wg0b{p}", [128, 16, 128], FP8) for p in range(2)]
    wg1b = [alloc(f"wg1b{p}", [128, 16, 128], FP8) for p in range(2)]
    gelb = [alloc(f"gelb{p}", [128, 512], BF16) for p in range(2)]

    wlb = [alloc("wl8a", [128, 64, 256], FP8, at=BASE),
           alloc("wl8b", [128, 64, 256], FP8, at=SW0)]
    off[0] = P4W0
    outst = [alloc(f"outst{p}", [128, 256], F32) for p in range(2)]
    xa5b = [alloc(f"xa5b{p}", [128, 256], F32) for p in range(2)]

    PSQ = [nc.alloc_psum_tensor(f"PSQ{i}", [128, 512], F32) for i in range(2)]
    PSS = [nc.alloc_psum_tensor(f"PSS{i}", [128, 512], F32) for i in range(2)]
    PT = [nc.alloc_psum_tensor(f"PT{i}", [128, 1024], BF16) for i in range(2)]
    PSO = nc.alloc_psum_tensor("PSO", [128, 512], F32)
    PSX = nc.alloc_psum_tensor("PSX", [128, 512], F32)

    pr = Prog(nc, es)
    st = pr.st
    pt_free = {0: None, 1: None}

    st("gpsimd", "dma_start", p8sb[:], d_p8[:], ctr="GK", amt=16)
    tok_consts = st("gpsimd", "dma_start", idsb[:], d_id[:], ctr="GK", amt=16)

    def emit_ln(xin, xout, sq, waits_dv, waits_ac):
        t_rsm = st("vector", "reduce_sum", ssum[:], xin[:], axis=AX, ctr="DV",
                   waits=waits_dv)
        t_ac = st("scalar", "activation", sq[:], xin[:], AF.Square, accum_out=ssq[:],
                  ctr="AC", waits=waits_ac)
        t_m = st("vector", "tensor_scalar", smean[:], ssum[:], 1.0 / D, None,
                 op0=ALU.mult, ctr="DV", waits=[("AC", t_ac), ("DV", t_rsm)])
        t_m2 = st("vector", "tensor_tensor", srstd[:], smean[:], smean[:], op=ALU.mult,
                  ctr="DV", waits=[("DV", t_m)])
        t_v = st("vector", "tensor_scalar", svar[:], ssq[:], 1.0 / D, srstd[:],
                 op0=ALU.mult, op1=ALU.subtract, ctr="DV", waits=[("DV", t_m2)])
        t_var = st("vector", "tensor_scalar", svar[:], svar[:], 1e-5, None, op0=ALU.add,
                   ctr="DV", waits=[("DV", t_v)])
        t_l = st("scalar", "activation", ssq[:], svar[:], AF.Ln, ctr="AC",
                 waits=[("DV", t_var)])
        t_e = st("scalar", "activation", srstd[:], ssq[:], AF.Exp, scale=-0.5,
                 ctr="AC", waits=[("AC", t_l)])
        t_out = st("vector", "tensor_scalar", xout[:], xin[:], smean[:], srstd[:],
                   op0=ALU.subtract, op1=ALU.mult, ctr="DV", waits=[("AC", t_e)])
        return ("DV", t_out)

    tg_rr = [0]

    def emit_tgroup(srcs, dst, wait_src, extra_te=(), extra_ev=(), evac_q="vector"):
        bi = tg_rr[0] % 2
        tg_rr[0] += 1
        bank = PT[bi]
        n = len(srcs)
        for ci, (s2, w) in enumerate(srcs):
            w_te = (list(wait_src) + [pt_free[bi], ("GK", tok_consts)]
                    + list(extra_te)) if ci == 0 else ()
            st("tensor", "matmul", bank[0:w, ci * 128:(ci + 1) * 128], s2, idsb[:],
               is_transpose=True, start=(ci == 0), stop=(ci == n - 1),
               ctr="TE" if ci == n - 1 else None, waits=w_te)
        t_te = pr.cnt["TE"]
        srcv = bank[0:128, 0:n * 128]
        if len(dst.shape) == 3:
            srcv = srcv.rearrange("p (a b) -> p a b", b=128)
        if evac_q == "vector":
            tok = st("vector", "tensor_copy", dst, srcv, ctr="DV",
                     waits=[("TE", t_te)] + list(extra_ev))
            pt_free[bi] = ("DV", tok)
        else:
            tok = st("scalar", "activation", dst, srcv, AF.Copy, ctr="AC",
                     waits=[("TE", t_te)] + list(extra_ev))
            pt_free[bi] = ("AC", tok)
        return pt_free[bi]

    last_qkv_te = None
    scores_done = {}
    vread_done = {}
    oscr_tok = {}
    prev = {}

    for sw in range(2):
        hh = sw * HH
        col0 = hh * DH
        w0 = [("TE", last_qkv_te)] if sw == 1 else []
        tok_w = None
        for dc in range(16):
            st("gpsimd", "dma_start", wqs[:, dc, :],
               d_wqT[dc * 128:(dc + 1) * 128, col0:col0 + 1024], ctr="GW", amt=16,
               waits=w0 if dc == 0 else ())
            st("gpsimd", "dma_start", wks[:, dc, :],
               d_wkT[dc * 128:(dc + 1) * 128, col0:col0 + 1024], ctr="GW", amt=16)
            tok_w = st("gpsimd", "dma_start", wvs[:, dc, :],
                       d_wvT[dc * 128:(dc + 1) * 128, col0:col0 + 1024], ctr="GW", amt=16)

        ln_of = {}

        def emit_xln(g2):
            p2 = g2 % 2
            xsrc2 = d_xo if g2 >= 16 else d_xp
            r2 = (g2 % 16) * 128
            t_x = st("sync", "dma_start", xbuf[p2][:], xsrc2[r2:r2 + 128, :],
                     ctr=f"SX{p2}", amt=16,
                     waits=[("DV", prev.get(f"xfree_dv_{p2}")),
                            ("AC", prev.get(f"xfree_ac_{p2}"))])
            prev[f"rc_tok_{p2}"] = st(
                "sync", "dma_start", rcb[p2][:], d_rc[g2 * 128:(g2 + 1) * 128, :],
                ctr=f"SR{p2}", amt=16, waits=[("DV", prev.get(f"rope_done_{p2}"))])
            prev[f"rs_tok_{p2}"] = st(
                "sync", "dma_start", rsb[p2][:], d_rs[g2 * 128:(g2 + 1) * 128, :],
                ctr=f"SR{p2}", amt=16)
            t_ln = emit_ln(xbuf[p2], xnb[p2], sqscr,
                           waits_dv=[(f"SX{p2}", t_x), ("TE", prev.get(f"xn_used_{p2}"))],
                           waits_ac=[(f"SX{p2}", t_x), ("DV", prev.get("ln_prev"))])
            prev[f"xfree_dv_{p2}"] = pr.cnt["DV"]
            prev[f"xfree_ac_{p2}"] = pr.cnt["AC"]
            prev["ln_prev"] = pr.cnt["DV"]
            ln_of[g2] = t_ln

        def emit_attention(cx, ck_bank_tok, cv_bank_tok):
            g, i, p, W = cx["g"], cx["i"], cx["p"], cx["W"]
            chains = {}
            normed = {}

            def scores_softmax(h):
                h3 = h % 3
                bank_c = PSS[0] if h % 2 == 0 else PSQ[0]
                bank_w = PSS[1] if h % 2 == 0 else PSQ[1]
                w_sc = [cx["t_qfeat"], prev.get(f"bfree_{bank_c.name}"),
                        ("AC", cx["t_ckev"]), ("AC", prev.get(f"exp_{h3}"))]
                st("tensor", "matmul", bank_c[:, 0:W], qft[p][:, h, :],
                   ck_st[:, hh + h, 0:W], start=True, stop=True, ctr="TE", waits=w_sc)
                w_sw = [prev.get(f"bfree_{bank_w.name}"), cx["t_kfeat"]]
                for j in range(3):
                    st("tensor", "matmul", bank_w[:, j * 128:(j + 1) * 128],
                       qft[p][:, h, :], kring[:, (g - 2 + j) % 4, h, :],
                       start=(j == 0), stop=(j == 2), ctr="TE",
                       waits=w_sw if j == 0 else ())
                t_sc = pr.cnt["TE"]
                scores_done[(sw, g)] = t_sc

                w_me = [("TE", t_sc), ("GM", pr.cnt["GM"]), ("AC", prev.get(f"exp_{h3}"))]
                tok_c = st("vector", "scalar_tensor_tensor", pbuf[h3][:, 0:W],
                           bank_c[:, 0:W], SCALE, mskb[p][:, 0:W],
                           op0=ALU.mult, op1=ALU.add, ctr="DV", waits=w_me)
                tok_mw = st("vector", "scalar_tensor_tensor", pbuf[h3][:, W:W + 384],
                            bank_w[:, 0:384], SCALE, mskb[p][:, W:W + 384],
                            op0=ALU.mult, op1=ALU.add, ctr="DV")
                prev[f"bfree_{bank_c.name}"] = ("DV", tok_c)
                prev[f"bfree_{bank_w.name}"] = ("DV", tok_mw)
                t_max = st("vector", "reduce_max", snmax[:, h3:h3 + 1],
                           pbuf[h3][:, 0:W + 384], axis=AX, negate=True, ctr="DV")
                t_exp = st("scalar", "activation", pexp[h3][:, 0:W + 384],
                           pbuf[h3][:, 0:W + 384], AF.Exp, bias=snmax[:, h3:h3 + 1],
                           accum_out=ssums[:, h3:h3 + 1], ctr="AC",
                           waits=[("DV", t_max), ("TE", prev.get(f"pexp_used_{h3}"))])
                prev[f"exp_{h3}"] = t_exp
                chains[h] = t_exp

            def norm_p(h):
                h3 = h % 3
                t_exp = chains.pop(h)
                t_rcp = st("vector", "reciprocal", srsum[:, h3:h3 + 1],
                           ssums[:, h3:h3 + 1], ctr="DV", waits=[("AC", t_exp)])
                t_pn = st("vector", "tensor_scalar", pexp[h3][:, 0:W + 384],
                          pexp[h3][:, 0:W + 384], srsum[:, h3:h3 + 1], None,
                          op0=ALU.mult, ctr="DV", waits=[("DV", t_rcp)])
                normed[h] = t_pn

            def tp_pv(h):
                h3 = h % 3
                t_pn = normed.pop(h)
                bank_o = PSO if h % 2 == 0 else PSX
                bounds = [(0, 128), (128, W), (W, W + 128),
                          (W + 128, W + 256), (W + 256, W + 384)]
                t_pt = emit_tgroup(
                    [(pexp[h3][:, a2:b2], b2 - a2) for (a2, b2) in bounds],
                    ptb[h % 2][:, :, :], [("DV", t_pn)],
                    extra_ev=[("TE", prev.get(f"ptb_used_{h % 2}"))],
                    evac_q="scalar" if h % 2 == 0 else "vector")
                prev[f"pexp_used_{h3}"] = pr.cnt["TE"]

                w_pv = [t_pt, prev.get(f"otfree_{bank_o.name}"),
                        ("GC", pr.cnt["GC"])]
                w_pv.append(("AC", ck_bank_tok) if h % 2 == 0 else ("AC", cv_bank_tok))
                vsrcs = [
                    (cv_st[0:128, 0, (hh + h) * 128:(hh + h + 1) * 128], 128, 0),
                    (cv_st[0:(W - 128), 1, (hh + h) * 128:(hh + h + 1) * 128], W - 128, 1),
                    (vring[:, (g - 2) % 4, h * 128:(h + 1) * 128], 128, 2),
                    (vring[:, (g - 1) % 4, h * 128:(h + 1) * 128], 128, 3),
                    (vring[:, g % 4, h * 128:(h + 1) * 128], 128, 4),
                ]
                for ci, (vs, kk, pc) in enumerate(vsrcs):
                    st("tensor", "matmul", bank_o[:, 0:128], vs, ptb[h % 2][0:kk, pc, :],
                       start=(ci == 0), stop=(ci == 4), ctr="TE",
                       waits=w_pv if ci == 0 else ())
                t_pv = pr.cnt["TE"]
                prev[f"ptb_used_{h % 2}"] = t_pv
                vread_done[(sw, g)] = t_pv
                w_oe = [("TE", t_pv)]
                if h == 0:
                    w_oe.append(("GS", oscr_tok.get((sw, i - 1))))
                t_oe = st("scalar", "activation", otile[:, h, :], bank_o[:, 0:128],
                          AF.Copy, ctr="AC", waits=w_oe)
                prev[f"otfree_{bank_o.name}"] = ("AC", t_oe)
                return t_oe

            for h in range(8):
                if h >= 3:
                    tp_pv(h - 3)
                scores_softmax(h)
                if h >= 1:
                    norm_p(h - 1)
            norm_p(7)
            tp_pv(5)
            tp_pv(6)
            last_oe = tp_pv(7)

            prev[f"mask_used_{p}"] = pr.cnt["DV"]
            oscr_tok[(sw, i)] = st(
                "gpsimd", "dma_start",
                d_oscr[i, hh:hh + 8].rearrange("c q t -> q c t"), otile[:],
                ctr="GS", amt=16, waits=[("AC", last_oe)])

        pend = None
        emit_xln(0)
        for g in range(NSW):
            own = g >= 16
            i = g - 16
            p = g % 2
            slot = g % 4
            t_rc = prev.get(f"rc_tok_{p}")
            t_rs = prev.get(f"rs_tok_{p}")
            t_m = None
            if own:
                W = 8 * g + 8
                t_m = st("gpsimd", "dma_start", mskb[p][:, 0:W + 384],
                         d_msk[i, :, 0:W + 384], ctr="GM", amt=16,
                         waits=[("DV", prev.get(f"mask_used_{p}"))])

            t_ln = ln_of.pop(g)

            emit_tgroup([(xnb[p][:, c * 128:(c + 1) * 128], 128) for c in range(8)],
                        xntb[:, 0:8, :], [t_ln])
            t_xnt = emit_tgroup(
                [(xnb[p][:, c * 128:(c + 1) * 128], 128) for c in range(8, 16)],
                xntb[:, 8:16, :], [t_ln])
            prev[f"xn_used_{p}"] = pr.cnt["TE"]

            def proj(wsb, out_aps, banks, extra=()):
                toks = []
                for cg in range(2):
                    bank = banks[cg]
                    w_mm = [t_xnt, ("GW", tok_w),
                            prev.get(f"bfree_{bank.name}")] + list(extra)
                    for dc in range(16):
                        st("tensor", "matmul", bank[:], xntb[:, dc, :],
                           wsb[:, dc, cg * 512:(cg + 1) * 512],
                           start=(dc == 0), stop=(dc == 15), ctr="TE",
                           waits=w_mm if dc == 0 else ())
                    tok = st("scalar", "activation", out_aps[cg], bank[:], AF.Copy,
                             ctr="AC", waits=[("TE", pr.cnt["TE"])])
                    prev[f"bfree_{bank.name}"] = ("AC", tok)
                    toks.append(tok)
                return toks[-1]

            t_q = None
            if own:
                t_q = proj(wqs, [qtok[p][:, 0:512], qtok[p][:, 512:1024]], PSQ)
            t_k = proj(wks, [ktok[p][:, 0:512], ktok[p][:, 512:1024]], PSS)
            t_v = proj(wvs, [vring[:, slot, 0:512], vring[:, slot, 512:1024]], PSQ,
                       extra=[("TE", vread_done.get((sw, g - 2)))])

            def rope(buf, tok_in):
                bv = buf[:].rearrange("q (h t) -> q h t", t=128)
                e, o = bv[:, :, 0:64], bv[:, :, 64:128]
                c = rcb[p][:].rearrange("q (h t) -> q h t", t=64)
                s_ = rsb[p][:].rearrange("q (h t) -> q h t", t=64)
                r1 = rs1[:].rearrange("q (h t) -> q h t", t=64)
                r2 = rs2[:].rearrange("q (h t) -> q h t", t=64)
                r3 = rs3[:].rearrange("q (h t) -> q h t", t=64)
                st("vector", "tensor_tensor", r1, e, c, op=ALU.mult, ctr="DV",
                   waits=[(f"SR{p}", t_rs), ("AC", tok_in)])
                st("vector", "tensor_tensor", r2, o, s_, op=ALU.mult, ctr="DV")
                st("vector", "tensor_tensor", r3, e, s_, op=ALU.mult, ctr="DV")
                st("vector", "tensor_tensor", e, r1, r2, op=ALU.subtract, ctr="DV")
                st("vector", "tensor_tensor", r1, o, c, op=ALU.mult, ctr="DV")
                return st("vector", "tensor_tensor", o, r3, r1, op=ALU.add, ctr="DV")

            t_ropek = rope(ktok[p], t_k)
            if own:
                t_ropeq = rope(qtok[p], t_q)
            prev[f"rope_done_{p}"] = pr.cnt["DV"]

            t_kfeat = prev.get("t_kfeat")
            if g >= 13:
                t_kfeat = emit_tgroup(
                    [(ktok[p][:, c * 128:(c + 1) * 128], 128) for c in range(8)],
                    kring[:, slot, :, :], [("DV", t_ropek)],
                    extra_ev=[("TE", scores_done.get((sw, g - 2)))])
                prev["t_kfeat"] = t_kfeat
            if own:
                t_qfeat = emit_tgroup(
                    [(qtok[p][:, c * 128:(c + 1) * 128], 128) for c in range(8)],
                    qft[p][:, :, :], [("DV", t_ropeq)],
                    extra_ev=[("TE", scores_done.get((sw, g - 2)))])

            w_pso = [("AC", prev.get("ck_evac")), prev.get("otfree_PSO"),
                     ("DV", t_ropek), ("GK", tok_consts)]
            for h in range(8):
                st("tensor", "matmul", PSO[:, 8 * h:8 * h + 8],
                   ktok[p][:, h * 128:(h + 1) * 128], p8sb[:],
                   start=(h == 0), stop=(h == 7), ctr="TE",
                   waits=w_pso if h == 0 else ())
            c0 = 8 * g
            t_ckev = st("scalar", "activation", ck_st[:, hh:hh + 8, c0:c0 + 8],
                        PSO[:, 0:64].rearrange("q (a b) -> q a b", b=8), AF.Copy,
                        ctr="AC", waits=[("TE", pr.cnt["TE"])])
            prev["ck_evac"] = t_ckev

            t1 = st("tensor", "matmul", PSX[0:8, 0:512], p8sb[:], vring[:, slot, 0:512],
                    start=True, stop=True, ctr="TE",
                    waits=[("AC", prev.get("cv_ev0")), prev.get("otfree_PSX"),
                           ("AC", t_v)])
            t2 = st("tensor", "matmul", PSS[0][0:8, 0:512], p8sb[:], vring[:, slot, 512:1024],
                    start=True, stop=True, ctr="TE",
                    waits=[("AC", prev.get("cv_ev1")), prev.get(f"bfree_{PSS[0].name}")])
            t_cv0 = st("scalar", "activation", cvstage[:, 0:512], PSX[0:8, 0:512], AF.Copy,
                       ctr="AC", waits=[("TE", t1), ("GC", prev.get("cv_moved"))])
            t_cv1 = st("scalar", "activation", cvstage[:, 512:1024], PSS[0][0:8, 0:512],
                       AF.Copy, ctr="AC", waits=[("TE", t2)])
            prev["cv_ev0"], prev["cv_ev1"] = t_cv0, t_cv1
            prev[f"bfree_{PSS[0].name}"] = ("AC", t_cv1)
            bt, rr = g // 16, 8 * (g % 16)
            t_cvmove = st("gpsimd", "dma_start",
                          cv_st[rr:rr + 8, bt, hh * 128:hh * 128 + 1024], cvstage[:],
                          ctr="GC", amt=16, waits=[("AC", t_cv1)])
            prev["cv_moved"] = t_cvmove

            if g + 1 < NSW:
                emit_xln(g + 1)

            if pend is not None:
                emit_attention(pend, t_ckev, t_cv0)
                pend = None
            if not own:
                scores_done[(sw, g)] = pr.cnt["TE"]
                vread_done[(sw, g)] = pr.cnt["TE"]
                continue
            pend = dict(g=g, i=i, p=p, W=W, t_m=t_m, t_qfeat=t_qfeat,
                        t_kfeat=t_kfeat, t_ckev=t_ckev, t_cvmove=t_cvmove)

        if pend is not None:
            emit_attention(pend, pend["t_ckev"], prev["cv_ev0"])
            pend = None

        last_qkv_te = pr.cnt["TE"]

    sweep_te, sweep_dv, sweep_ac = pr.cnt["TE"], pr.cnt["DV"], pr.cnt["AC"]
    sweep_dmas = [("SX0", pr.cnt["SX0"]), ("SX1", pr.cnt["SX1"]),
                  ("SR0", pr.cnt["SR0"]), ("SR1", pr.cnt["SR1"]),
                  ("GC", pr.cnt["GC"]), ("GS", pr.cnt["GS"])]
    if "dbg_ck" in dbg:
        st("sync", "dma_start", dbg["dbg_ck"], ck_st[:], ctr="ST", amt=16,
           waits=[("AC", sweep_ac)])
    if "dbg_cv" in dbg:
        st("sync", "dma_start", dbg["dbg_cv"], cv_st[:], ctr="ST", amt=16,
           waits=[("GC", pr.cnt["GC"])])

    w_bar = [("TE", sweep_te), ("DV", sweep_dv), ("AC", sweep_ac)] + sweep_dmas
    tok_wo = None
    for c in range(16):
        tok_wo = st("gpsimd", "dma_start", wos[:, c, :],
                    d_woT[c * 128:(c + 1) * 128, :], ctr="GW", amt=16,
                    waits=w_bar if c == 0 else ())

    prev3 = {}
    xa_spill = {}

    def emit_ln2(i):
        p = i % 2
        t_xa = prev3[f"xa_tok_{p}"]
        t_ln2 = emit_ln(xab[p], xn2b[p], sq3,
                        waits_dv=[("DV", t_xa), ("TE", prev3.get(f"xn2_used_{p}"))],
                        waits_ac=[("DV", t_xa), ("DV", prev3.get("ln2_out"))])
        prev3["ln2_out"] = pr.cnt["DV"]
        emit_tgroup([(xn2b[p][:, c * 128:(c + 1) * 128], 128) for c in range(8)],
                    xn2t[:, 0:8, i * 128:(i + 1) * 128], [t_ln2])
        emit_tgroup([(xn2b[p][:, c * 128:(c + 1) * 128], 128) for c in range(8, 16)],
                    xn2t[:, 8:16, i * 128:(i + 1) * 128], [t_ln2])
        prev3[f"xn2_used_{p}"] = pr.cnt["TE"]

    for i in range(NT):
        p = i % 2
        t_o = st("sync", "dma_start", o3b[p][:], d_oscr[i].rearrange("c q t -> q c t"),
                 ctr="S3A", amt=16,
                 waits=[("GS", pr.cnt["GS"]),
                        ("TE", prev3.get(f"o_used_{p}")), ("TE", sweep_te)])
        t_x3 = st("sync", "dma_start", x3b[p][:], d_xo[i * 128:(i + 1) * 128, :],
                  ctr="S3B", amt=16, waits=[("DV", prev3.get(f"x3_used_{p}"))])
        for q in range(4):
            bank = PSQ[q % 2]
            w_mm = [("S3A", t_o), ("GW", tok_wo), ("DV", prev3.get(f"b3_{bank.name}"))]
            for c in range(16):
                st("tensor", "matmul", bank[:], o3b[p][:, c, :],
                   wos[:, c, q * 512:(q + 1) * 512], start=(c == 0), stop=(c == 15),
                   ctr="TE", waits=w_mm if c == 0 else ())
            w_ev = [("TE", pr.cnt["TE"]), ("S3B", t_x3)]
            if q == 0 and i >= 2:
                w_ev.append(("SSP", pr.cnt["SSP"]))
            tok = st("vector", "tensor_tensor", xab[p][:, q * 512:(q + 1) * 512],
                     bank[:], x3b[p][:, q * 512:(q + 1) * 512], op=ALU.add,
                     ctr="DV", waits=w_ev)
            prev3[f"b3_{bank.name}"] = tok
        prev3[f"o_used_{p}"] = pr.cnt["TE"]
        prev3[f"x3_used_{p}"] = pr.cnt["DV"]
        t_xa = prev3[f"b3_{PSQ[1].name}"]
        prev3[f"xa_tok_{p}"] = t_xa
        xa_spill[i] = st("sync", "dma_start", d_xascr[i * 128:(i + 1) * 128, :],
                         xab[p][:], ctr="SSP", amt=16, waits=[("DV", t_xa)])
        if i >= 1:
            emit_ln2(i - 1)
    emit_ln2(NT - 1)

    p3_te, p3_dv, p3_ac = pr.cnt["TE"], pr.cnt["DV"], pr.cnt["AC"]
    p3_dmas = [("S3A", pr.cnt["S3A"]), ("S3B", pr.cnt["S3B"]), ("SSP", pr.cnt["SSP"])]
    if "dbg_xa" in dbg:
        st("sync", "dma_start", dbg["dbg_xa"], d_xascr[0:128, :], ctr="ST", amt=16,
           waits=[("SSP", pr.cnt["SSP"])])

    bar4 = [("TE", p3_te), ("DV", p3_dv), ("AC", p3_ac)] + p3_dmas
    prev4 = {}
    for hc in range(64):
        bp = hc % 2
        t_w0 = st("gpsimd", "dma_start", wg0b[bp][:], d_wg0[hc], ctr=f"GA{bp}", amt=16,
                  waits=[("TE", prev4.get(f"wg_used_{bp}"))] + (bar4 if hc == 0 else []))
        t_w1 = st("gpsimd", "dma_start", wg1b[bp][:], d_wg1[hc], ctr=f"GA{bp}", amt=16)
        for ts in range(4):
            b0 = PSQ[0] if ts % 2 == 0 else PSS[0]
            b1 = PSQ[1] if ts % 2 == 0 else PSS[1]
            gp_ = ts % 2
            w_g0 = [(f"GA{bp}", t_w1), ("AC", prev4.get(f"gel_{b0.name}")),
                    ("DV", prev4.get(f"mul_{b1.name}"))]
            if hc == 0 and ts == 0:
                w_g0 = w_g0 + bar4
            for dc in range(8):
                st("tensor", "matmul", b0[:], wg0b[bp][:, 2 * dc:2 * dc + 2, :],
                   xn2t[:, 2 * dc:2 * dc + 2, ts * 512:(ts + 1) * 512],
                   start=(dc == 0), stop=(dc == 7), ctr="TE",
                   perf_mode=mybir.MatmulPerfMode.DoubleRow,
                   waits=w_g0 if dc == 0 else ())
            t_g0 = pr.cnt["TE"]
            for dc in range(8):
                st("tensor", "matmul", b1[:], wg1b[bp][:, 2 * dc:2 * dc + 2, :],
                   xn2t[:, 2 * dc:2 * dc + 2, ts * 512:(ts + 1) * 512],
                   start=(dc == 0), stop=(dc == 7), ctr="TE",
                   perf_mode=mybir.MatmulPerfMode.DoubleRow)
            t_g1 = pr.cnt["TE"]
            t_gel = st("scalar", "activation", gelb[gp_][:], b0[:], AF.Gelu_apprx_tanh,
                       ctr="AC", waits=[("TE", t_g0), ("DV", prev4.get(f"gelu_{gp_}"))])
            prev4[f"gel_{b0.name}"] = t_gel
            t_mul = st("vector", "tensor_tensor", act[:, hc, ts * 512:(ts + 1) * 512],
                       gelb[gp_][:], b1[:], op=ALU.mult, ctr="DV",
                       waits=[("TE", t_g1), ("AC", t_gel)])
            prev4[f"mul_{b1.name}"] = t_mul
            prev4[f"gelu_{gp_}"] = t_mul
        prev4[f"wg_used_{bp}"] = pr.cnt["TE"]

    p4_te, p4_dv, p4_ac = pr.cnt["TE"], pr.cnt["DV"], pr.cnt["AC"]
    if "dbg_act" in dbg:
        st("sync", "dma_start", dbg["dbg_act"], act[:], ctr="ST", amt=16,
           waits=[("DV", p4_dv)])
    if "dbg_xn2t" in dbg:
        st("sync", "dma_start", dbg["dbg_xn2t"], xn2t[:], ctr="ST", amt=16,
           waits=[("DV", p3_dv)])

    prev5 = {}
    d_wl_v = d_wl.rearrange("(c p) m -> p c m", p=128)
    for q in range(8):
        bi_ = q % 2
        buf = wlb[bi_]
        if q == 0:
            w_wl = [("TE", sweep_te)]
        elif q == 1:
            w_wl = [("TE", p4_te)]
        else:
            w_wl = [("TE", prev5.get(f"wl_used_{bi_}"))]
        tok_wl = None
        for hg in range(8):
            tok_wl = st("gpsimd", "dma_start", buf[:, hg * 8:(hg + 1) * 8, :],
                        d_wl_v[:, hg * 8:(hg + 1) * 8, q * 256:(q + 1) * 256],
                        ctr=f"WL{bi_}", amt=16, waits=w_wl if hg == 0 else ())
        for i in range(NT):
            p = i % 2
            bank = PSQ[i % 2]
            t_xa5 = st("sync", "dma_start", xa5b[p][:],
                       d_xascr[i * 128:(i + 1) * 128, q * 256:(q + 1) * 256],
                       ctr="SA5", amt=16,
                       waits=[("DV", prev5.get(f"xa5_used_{p}")), ("SSP", pr.cnt["SSP"]),
                              ("AC", p4_ac)])
            w_mm = [(f"WL{bi_}", tok_wl), ("DV", prev5.get(f"b5_{bank.name}")), ("DV", p4_dv)]
            for hc in range(32):
                st("tensor", "matmul", bank[:, 0:256],
                   act[:, 2 * hc:2 * hc + 2, i * 128:(i + 1) * 128],
                   buf[:, 2 * hc:2 * hc + 2, :],
                   start=(hc == 0), stop=(hc == 31), ctr="TE",
                   perf_mode=mybir.MatmulPerfMode.DoubleRow,
                   waits=w_mm if hc == 0 else ())
            t_dn = pr.cnt["TE"]
            op_ = i % 2
            t_ev = st("vector", "tensor_tensor", outst[op_][:], bank[:, 0:256],
                      xa5b[p][:], op=ALU.add, ctr="DV",
                      waits=[("TE", t_dn), ("SA5", t_xa5),
                             ("ST", pr.cnt["ST"])])
            prev5[f"b5_{bank.name}"] = t_ev
            prev5[f"xa5_used_{p}"] = t_ev
            tok_out = st("sync", "dma_start",
                         d_y[i * 128:(i + 1) * 128, q * 256:(q + 1) * 256],
                         outst[op_][:], ctr="ST", amt=16, waits=[("DV", t_ev)])
            prev5[f"odma_{op_}"] = tok_out
        prev5[f"wl_used_{bi_}"] = pr.cnt["TE"]

    st("sync", None, waits=[("ST", pr.cnt["ST"])])

    with nc.Block() as block:
        pr.run(block)
    es.close()
    return nc


_CACHE = {}


def _host_inputs(x, wq, wk, wv, wo, wg, wl):
    x = np.ascontiguousarray(np.asarray(x, dtype=np.float32))
    wq = np.asarray(wq, dtype=np.float32)
    wk = np.asarray(wk, dtype=np.float32)
    wv = np.asarray(wv, dtype=np.float32)
    wo = np.asarray(wo, dtype=np.float32)
    wg = np.asarray(wg, dtype=np.float32)
    wl = np.ascontiguousarray(np.asarray(wl, dtype=np.float32))

    perm = np.concatenate([np.concatenate([h * 128 + np.arange(0, 128, 2),
                                           h * 128 + np.arange(1, 128, 2)])
                           for h in range(16)])
    wqT = np.ascontiguousarray(wq.T[:, perm])
    wkT = np.ascontiguousarray(wk.T[:, perm])
    wvT = np.ascontiguousarray(wv.T)
    woT = np.ascontiguousarray(wo.T)
    wg0t = np.ascontiguousarray(wg[0].reshape(16, 128, 64, 128).transpose(2, 1, 0, 3))
    wg1t = np.ascontiguousarray(wg[1].reshape(16, 128, 64, 128).transpose(2, 1, 0, 3))

    inv = 1.0 / 10000.0 ** (np.arange(0, DH, 2, dtype=np.float32) / DH)
    f = np.outer(np.arange(S, dtype=np.float32), inv)
    fc = np.exp(1j * f).astype(np.complex64)
    cosg = np.tile(np.real(fc).astype(np.float32), (1, 8))
    sing = np.tile(np.imag(fc).astype(np.float32), (1, 8))

    p8c = np.zeros((128, 8), dtype=np.float32)
    for t in range(128):
        p8c[t, t // 16] = 1.0 / 16.0
    identc = np.eye(128, dtype=np.float32)

    cores = []
    for bi in range(B):
        for half in range(2):
            t0 = 2048 * half
            xp = np.zeros((2048, D), dtype=np.float32) if half == 0 \
                else np.ascontiguousarray(x[bi, 0:2048])
            xo = np.ascontiguousarray(x[bi, t0:t0 + 2048])
            rc = np.zeros((4096, 512), dtype=np.float32)
            rs = np.zeros((4096, 512), dtype=np.float32)
            if half == 0:
                rc[2048:], rs[2048:] = cosg[0:2048], sing[0:2048]
            else:
                rc[:], rs[:] = cosg, sing
            masks = np.full((NT, 128, 640), NEG, dtype=np.float32)
            qi = np.arange(128)[:, None]
            m = np.arange(128)[None, :]
            for i in range(NT):
                T_abs = t0 + 128 * i
                W = 8 * (16 + i) + 8
                jj = np.arange(W)[None, :]
                blk_abs = t0 // 16 - 128 + jj
                ok = (blk_abs >= 0) & (blk_abs < (T_abs + qi + 1) // 16)
                masks[i, :, 0:W] = np.where(ok, 0.0, NEG)
                k0 = ((T_abs - 256 + m) >= 0) & (qi < m)
                masks[i, :, W:W + 128] = np.where(k0, 0.0, NEG)
                k1 = np.broadcast_to((T_abs - 128 + m) >= 0, (128, 128))
                masks[i, :, W + 128:W + 256] = np.where(k1, 0.0, NEG)
                masks[i, :, W + 256:W + 384] = np.where(m <= qi, 0.0, NEG)
            cores.append(dict(xp=xp, xo=xo, wqT=wqT, wkT=wkT, wvT=wvT, woT=woT,
                              wg0t=wg0t, wg1t=wg1t, wl=wl, ropec=rc, ropes=rs,
                              masks=masks, p8c=p8c, identc=identc))
    return cores


def kernel(L0_attn_x, L0_attn_wq, L0_attn_wk, L0_attn_wv, L0_attn_wo,
           L0_ffn_w_gating, L0_ffn_w_linear):
    cores = _host_inputs(L0_attn_x, L0_attn_wq, L0_attn_wk, L0_attn_wv,
                         L0_attn_wo, L0_ffn_w_gating, L0_ffn_w_linear)
    if "nc" not in _CACHE:
        _CACHE["nc"] = build_program()
    res = run_bass_kernel_spmd(_CACHE["nc"], cores, core_ids=list(range(8)))
    out = np.empty((B, S, D), dtype=np.float32)
    for bi in range(B):
        for half in range(2):
            out[bi, 2048 * half:2048 * (half + 1)] = res.results[bi * 2 + half]["y"]
    return out


# version 19
# speedup vs baseline: 1.0134x; 1.0134x over previous
import contextlib
import numpy as np
import ml_dtypes
import concourse.bass as bass
import concourse.mybir as mybir
from concourse.bass_utils import run_bass_kernel_spmd

F32 = mybir.dt.float32
BF16 = mybir.dt.bfloat16
FP8 = mybir.dt.float8e4
AF = mybir.ActivationFunctionType
ALU = mybir.AluOpType
AX = mybir.AxisListType.X

B, S, D = 4, 4096, 2048
H, DH, HH = 16, 128, 8
NT = 16
NSW = 32
HID = 8192
SCALE = float(1.0 / np.sqrt(DH))
NEG = -1e30
DBG_TILE = 0


class Prog:

    ENGS = ("tensor", "vector", "scalar", "sync", "gpsimd")

    def __init__(self, nc, es):
        self.nc = nc
        self.qs = {e: [] for e in self.ENGS}
        self.sems = {}
        self.cnt = {}
        self.waited = {e: {} for e in self.ENGS}
        for name in ("TE", "DV", "AC", "ST", "SX0", "SX1", "SR0", "SR1", "GM",
                     "GW", "GK", "GC", "GS", "S3A0", "S3A1", "S3B0", "S3B1", "SSP", "SA5",
                     "GA0", "GA1", "WL0", "WL1"):
            self.sems[name] = es.enter_context(nc.semaphore(name))
            self.cnt[name] = 0

    def st(self, q, meth, *args, ctr=None, amt=1, waits=(), **kwargs):
        tok = None
        if ctr is not None:
            self.cnt[ctr] += amt
            tok = self.cnt[ctr]
        wt = []
        for cv in waits:
            if cv is None:
                continue
            c, v = cv
            if v is None or v <= 0:
                continue
            if self.waited[q].get(c, 0) >= v:
                continue
            assert v <= self.cnt[c], f"wait on future {c}={v} (cnt {self.cnt[c]}) in {q}"
            self.waited[q][c] = v
            wt.append((c, v))
        self.qs[q].append((tuple(wt), meth, args, kwargs, ctr, amt))
        return tok

    def run(self, block):
        sems = self.sems

        def runner(eng, ops):
            for wt, meth, args, kwargs, ctr, amt in ops:
                for c, v in wt:
                    eng.wait_ge(sems[c], v)
                if meth is None:
                    continue
                ins = getattr(eng, meth)(*args, **kwargs)
                if ctr is not None:
                    ins.then_inc(sems[ctr], amt)
        for name in self.ENGS:
            getattr(block, name)(lambda eng, ops=tuple(self.qs[name]): runner(eng, ops))


def build_program(debug=()):
    nc = bass.Bass("TRN2", target_bir_lowering=False)
    es = contextlib.ExitStack()

    d_xp = nc.dram_tensor("xp", [2048, D], F32, kind="ExternalInput").ap()
    d_xo = nc.dram_tensor("xo", [2048, D], F32, kind="ExternalInput").ap()
    d_wqT = nc.dram_tensor("wqT", [D, D], F32, kind="ExternalInput").ap()
    d_wkT = nc.dram_tensor("wkT", [D, D], F32, kind="ExternalInput").ap()
    d_wvT = nc.dram_tensor("wvT", [D, D], F32, kind="ExternalInput").ap()
    d_woT = nc.dram_tensor("woT", [D, D], F32, kind="ExternalInput").ap()
    d_wg0 = nc.dram_tensor("wg0t", [64, 128, 16, 128], F32, kind="ExternalInput").ap()
    d_wg1 = nc.dram_tensor("wg1t", [64, 128, 16, 128], F32, kind="ExternalInput").ap()
    d_wl = nc.dram_tensor("wl", [HID, D], F32, kind="ExternalInput").ap()
    d_rc = nc.dram_tensor("ropec", [4096, 512], F32, kind="ExternalInput").ap()
    d_rs = nc.dram_tensor("ropes", [4096, 512], F32, kind="ExternalInput").ap()
    d_msk = nc.dram_tensor("masks", [NT, 128, 640], F32, kind="ExternalInput").ap()
    d_p8 = nc.dram_tensor("p8c", [128, 8], F32, kind="ExternalInput").ap()
    d_id = nc.dram_tensor("identc", [128, 128], F32, kind="ExternalInput").ap()
    d_y = nc.dram_tensor("y", [2048, D], F32, kind="ExternalOutput").ap()
    d_oscr = nc.dram_tensor("o_scr", [NT, 16, 128, 128], BF16, kind="Internal").ap()
    d_xascr = nc.dram_tensor("xa_scr", [2048, D], F32, kind="Internal").ap()
    dbg = {}
    for nm, shp, dt in debug:
        dbg[nm] = nc.dram_tensor(nm, shp, dt, kind="ExternalOutput").ap()

    BASE = (nc._sbuf_addr_for_side("left") + 31) // 32 * 32
    LIMIT = 229376 - 64
    off = [BASE]

    def alloc(name, shape, dtype, at=None):
        if at is None:
            at = off[0]
            sz = int(np.prod(shape[1:])) * mybir.dt.size(dtype)
            off[0] = (at + sz + 31) // 32 * 32
        end = at + int(np.prod(shape[1:])) * mybir.dt.size(dtype)
        assert end <= LIMIT, (name, end, LIMIT)
        return nc.alloc_sbuf_tensor_at(name, shape, dtype, offset=at)

    ck_st = alloc("ck_st", [128, 16, 256], BF16)
    cv_st = alloc("cv_st", [128, 2, 2048], BF16)
    p8sb = alloc("p8sb", [128, 8], BF16)
    idsb = alloc("idsb", [128, 128], BF16)
    ssum = alloc("ssum", [128, 1], F32)
    ssq = alloc("ssq", [128, 1], F32)
    smean = alloc("smean", [128, 1], F32)
    svar = alloc("svar", [128, 1], F32)
    srstd = alloc("srstd", [128, 1], F32)
    snmax = alloc("snmax", [128, 3], F32)
    ssums = alloc("ssums", [128, 3], F32)
    srsum = alloc("srsum", [128, 3], F32)
    snmr = alloc("snmr", [128, 1], F32)
    P_END = off[0]

    SW0 = P_END
    off[0] = SW0
    xbuf = [alloc(f"xbuf{p}", [128, 2048], F32) for p in range(2)]
    sqscr = alloc("sqscr", [128, 2048], F32)
    xnb = [alloc(f"xn{p}", [128, 2048], BF16) for p in range(2)]
    xntb = alloc("xnt", [128, 16, 128], BF16)
    rcb = [alloc(f"rc{p}", [128, 512], F32) for p in range(2)]
    rsb = [alloc(f"rs{p}", [128, 512], F32) for p in range(2)]
    qtok = [alloc(f"qtok{p}", [128, 1024], BF16) for p in range(2)]
    ktok = [alloc(f"ktok{p}", [128, 1024], BF16) for p in range(2)]
    rs1 = alloc("rs1", [128, 512], BF16)
    rs2 = alloc("rs2", [128, 512], BF16)
    rs3 = alloc("rs3", [128, 512], BF16)
    qft = [alloc(f"qft{p}", [128, 8, 128], BF16) for p in range(2)]
    mskb = [alloc(f"mskb{p}", [128, 640], BF16) for p in range(2)]
    pbuf = [alloc(f"pbuf{p}", [128, 640], F32) for p in range(3)]
    pexp = [alloc(f"pexp{p}", [128, 640], BF16) for p in range(3)]
    ptb = [alloc(f"ptb{p}", [128, 5, 128], BF16) for p in range(2)]
    otile = alloc("otile", [128, 8, 128], BF16)
    cvstage = alloc("cvstage", [8, 1024], BF16)
    kring = alloc("kring", [128, 4, 8, 128], BF16)
    vring = alloc("vring", [128, 4, 1024], BF16)
    wqs = alloc("wqs", [128, 16, 1024], BF16)
    wks = alloc("wks", [128, 16, 1024], BF16)
    wvs = alloc("wvs", [128, 16, 1024], BF16)

    off[0] = SW0
    xn2t = alloc("xn2t", [128, 16, 2048], FP8)
    P3W0 = off[0]
    wos = alloc("wos", [128, 16, 2048], BF16)
    o3b = [alloc(f"o3b{p}", [128, 16, 128], BF16) for p in range(2)]
    x3b = [alloc(f"x3b{p}", [128, 2048], F32) for p in range(2)]
    xab = [alloc(f"xab{p}", [128, 2048], F32) for p in range(2)]
    xn2b = [alloc(f"xn2b{p}", [128, 2048], BF16) for p in range(2)]
    sq3 = alloc("sq3", [128, 2048], F32)

    off[0] = P3W0
    act = alloc("act", [128, 64, 2048], FP8)
    P4W0 = off[0]
    wg0b = [alloc(f"wg0b{p}", [128, 16, 128], FP8) for p in range(2)]
    wg1b = [alloc(f"wg1b{p}", [128, 16, 128], FP8) for p in range(2)]
    gelb = [alloc(f"gelb{p}", [128, 512], BF16) for p in range(2)]

    wlb = [alloc("wl8a", [128, 64, 256], FP8, at=BASE),
           alloc("wl8b", [128, 64, 256], FP8, at=SW0)]
    off[0] = P4W0
    outst = [alloc(f"outst{p}", [128, 256], F32) for p in range(2)]
    xa5b = [alloc(f"xa5b{p}", [128, 256], F32) for p in range(2)]

    PSQ = [nc.alloc_psum_tensor(f"PSQ{i}", [128, 512], F32) for i in range(2)]
    PSS = [nc.alloc_psum_tensor(f"PSS{i}", [128, 512], F32) for i in range(2)]
    PT = [nc.alloc_psum_tensor(f"PT{i}", [128, 1024], BF16) for i in range(2)]
    PSO = nc.alloc_psum_tensor("PSO", [128, 512], F32)
    PSX = nc.alloc_psum_tensor("PSX", [128, 512], F32)

    pr = Prog(nc, es)
    st = pr.st
    pt_free = {0: None, 1: None}

    st("gpsimd", "dma_start", p8sb[:], d_p8[:], ctr="GK", amt=16)
    tok_consts = st("gpsimd", "dma_start", idsb[:], d_id[:], ctr="GK", amt=16)

    def emit_ln(xin, xout, sq, waits_dv, waits_ac):
        t_rsm = st("vector", "reduce_sum", ssum[:], xin[:], axis=AX, ctr="DV",
                   waits=waits_dv)
        t_ac = st("scalar", "activation", sq[:], xin[:], AF.Square, accum_out=ssq[:],
                  ctr="AC", waits=waits_ac)
        t_m = st("vector", "tensor_scalar", smean[:], ssum[:], 1.0 / D, None,
                 op0=ALU.mult, ctr="DV", waits=[("AC", t_ac), ("DV", t_rsm)])
        t_m2 = st("vector", "tensor_tensor", srstd[:], smean[:], smean[:], op=ALU.mult,
                  ctr="DV", waits=[("DV", t_m)])
        t_v = st("vector", "tensor_scalar", svar[:], ssq[:], 1.0 / D, srstd[:],
                 op0=ALU.mult, op1=ALU.subtract, ctr="DV", waits=[("DV", t_m2)])
        t_var = st("vector", "tensor_scalar", svar[:], svar[:], 1e-5, None, op0=ALU.add,
                   ctr="DV", waits=[("DV", t_v)])
        t_l = st("scalar", "activation", ssq[:], svar[:], AF.Ln, ctr="AC",
                 waits=[("DV", t_var)])
        t_e = st("scalar", "activation", srstd[:], ssq[:], AF.Exp, scale=-0.5,
                 ctr="AC", waits=[("AC", t_l)])
        t_out = st("vector", "tensor_scalar", xout[:], xin[:], smean[:], srstd[:],
                   op0=ALU.subtract, op1=ALU.mult, ctr="DV", waits=[("AC", t_e)])
        return ("DV", t_out)

    tg_rr = [0]

    def emit_tgroup(srcs, dst, wait_src, extra_te=(), extra_ev=(), evac_q="vector"):
        bi = tg_rr[0] % 2
        tg_rr[0] += 1
        bank = PT[bi]
        n = len(srcs)
        for ci, (s2, w) in enumerate(srcs):
            w_te = (list(wait_src) + [pt_free[bi], ("GK", tok_consts)]
                    + list(extra_te)) if ci == 0 else ()
            st("tensor", "matmul", bank[0:w, ci * 128:(ci + 1) * 128], s2, idsb[:],
               is_transpose=True, start=(ci == 0), stop=(ci == n - 1),
               ctr="TE" if ci == n - 1 else None, waits=w_te)
        t_te = pr.cnt["TE"]
        srcv = bank[0:128, 0:n * 128]
        if len(dst.shape) == 3:
            srcv = srcv.rearrange("p (a b) -> p a b", b=128)
        if evac_q == "vector":
            tok = st("vector", "tensor_copy", dst, srcv, ctr="DV",
                     waits=[("TE", t_te)] + list(extra_ev))
            pt_free[bi] = ("DV", tok)
        else:
            tok = st("scalar", "activation", dst, srcv, AF.Copy, ctr="AC",
                     waits=[("TE", t_te)] + list(extra_ev))
            pt_free[bi] = ("AC", tok)
        return pt_free[bi]

    last_qkv_te = None
    scores_done = {}
    vread_done = {}
    oscr_tok = {}
    prev = {}

    for sw in range(2):
        hh = sw * HH
        col0 = hh * DH
        w0 = [("TE", last_qkv_te)] if sw == 1 else []
        tok_w = None
        for dc in range(16):
            st("gpsimd", "dma_start", wqs[:, dc, :],
               d_wqT[dc * 128:(dc + 1) * 128, col0:col0 + 1024], ctr="GW", amt=16,
               waits=w0 if dc == 0 else ())
            st("gpsimd", "dma_start", wks[:, dc, :],
               d_wkT[dc * 128:(dc + 1) * 128, col0:col0 + 1024], ctr="GW", amt=16)
            tok_w = st("gpsimd", "dma_start", wvs[:, dc, :],
                       d_wvT[dc * 128:(dc + 1) * 128, col0:col0 + 1024], ctr="GW", amt=16)

        ln_of = {}

        def emit_xln(g2):
            p2 = g2 % 2
            xsrc2 = d_xo if g2 >= 16 else d_xp
            r2 = (g2 % 16) * 128
            t_x = st("sync", "dma_start", xbuf[p2][:], xsrc2[r2:r2 + 128, :],
                     ctr=f"SX{p2}", amt=16,
                     waits=[("DV", prev.get(f"xfree_dv_{p2}")),
                            ("AC", prev.get(f"xfree_ac_{p2}"))])
            prev[f"rc_tok_{p2}"] = st(
                "sync", "dma_start", rcb[p2][:], d_rc[g2 * 128:(g2 + 1) * 128, :],
                ctr=f"SR{p2}", amt=16, waits=[("DV", prev.get(f"rope_done_{p2}"))])
            prev[f"rs_tok_{p2}"] = st(
                "sync", "dma_start", rsb[p2][:], d_rs[g2 * 128:(g2 + 1) * 128, :],
                ctr=f"SR{p2}", amt=16)
            t_ln = emit_ln(xbuf[p2], xnb[p2], sqscr,
                           waits_dv=[(f"SX{p2}", t_x), ("TE", prev.get(f"xn_used_{p2}"))],
                           waits_ac=[(f"SX{p2}", t_x), ("DV", prev.get("ln_prev"))])
            prev[f"xfree_dv_{p2}"] = pr.cnt["DV"]
            prev[f"xfree_ac_{p2}"] = pr.cnt["AC"]
            prev["ln_prev"] = pr.cnt["DV"]
            ln_of[g2] = t_ln

        def emit_attention(cx, ck_bank_tok, cv_bank_tok):
            g, i, p, W = cx["g"], cx["i"], cx["p"], cx["W"]
            chains = {}
            normed = {}

            def scores_softmax(h):
                h3 = h % 3
                bank_c = PSS[0] if h % 2 == 0 else PSQ[0]
                bank_w = PSS[1] if h % 2 == 0 else PSQ[1]
                w_sc = [cx["t_qfeat"], prev.get(f"bfree_{bank_c.name}"),
                        ("AC", cx["t_ckev"]), ("AC", prev.get(f"exp_{h3}"))]
                st("tensor", "matmul", bank_c[:, 0:W], qft[p][:, h, :],
                   ck_st[:, hh + h, 0:W], start=True, stop=True, ctr="TE", waits=w_sc)
                w_sw = [prev.get(f"bfree_{bank_w.name}"), cx["t_kfeat"]]
                for j in range(3):
                    st("tensor", "matmul", bank_w[:, j * 128:(j + 1) * 128],
                       qft[p][:, h, :], kring[:, (g - 2 + j) % 4, h, :],
                       start=(j == 0), stop=(j == 2), ctr="TE",
                       waits=w_sw if j == 0 else ())
                t_sc = pr.cnt["TE"]
                scores_done[(sw, g)] = t_sc

                w_me = [("TE", t_sc), ("GM", pr.cnt["GM"]), ("AC", prev.get(f"exp_{h3}"))]
                tok_c = st("vector", "scalar_tensor_tensor", pbuf[h3][:, 0:W],
                           bank_c[:, 0:W], SCALE, mskb[p][:, 0:W],
                           op0=ALU.mult, op1=ALU.add, ctr="DV", waits=w_me)
                tok_mw = st("vector", "scalar_tensor_tensor", pbuf[h3][:, W:W + 384],
                            bank_w[:, 0:384], SCALE, mskb[p][:, W:W + 384],
                            op0=ALU.mult, op1=ALU.add, ctr="DV")
                prev[f"bfree_{bank_c.name}"] = ("DV", tok_c)
                prev[f"bfree_{bank_w.name}"] = ("DV", tok_mw)
                t_max = st("vector", "reduce_max", snmax[:, h3:h3 + 1],
                           pbuf[h3][:, 0:W + 384], axis=AX, negate=True, ctr="DV")
                t_exp = st("scalar", "activation", pexp[h3][:, 0:W + 384],
                           pbuf[h3][:, 0:W + 384], AF.Exp, bias=snmax[:, h3:h3 + 1],
                           accum_out=ssums[:, h3:h3 + 1], ctr="AC",
                           waits=[("DV", t_max), ("TE", prev.get(f"pexp_used_{h3}"))])
                prev[f"exp_{h3}"] = t_exp
                chains[h] = t_exp

            def norm_p(h):
                h3 = h % 3
                t_exp = chains.pop(h)
                t_rcp = st("vector", "reciprocal", srsum[:, h3:h3 + 1],
                           ssums[:, h3:h3 + 1], ctr="DV", waits=[("AC", t_exp)])
                t_pn = st("vector", "tensor_scalar", pexp[h3][:, 0:W + 384],
                          pexp[h3][:, 0:W + 384], srsum[:, h3:h3 + 1], None,
                          op0=ALU.mult, ctr="DV", waits=[("DV", t_rcp)])
                normed[h] = t_pn

            def tp_pv(h):
                h3 = h % 3
                t_pn = normed.pop(h)
                bank_o = PSO if h % 2 == 0 else PSX
                bounds = [(0, 128), (128, W), (W, W + 128),
                          (W + 128, W + 256), (W + 256, W + 384)]
                t_pt = emit_tgroup(
                    [(pexp[h3][:, a2:b2], b2 - a2) for (a2, b2) in bounds],
                    ptb[h % 2][:, :, :], [("DV", t_pn)],
                    extra_ev=[("TE", prev.get(f"ptb_used_{h % 2}"))],
                    evac_q="scalar" if h % 2 == 0 else "vector")
                prev[f"pexp_used_{h3}"] = pr.cnt["TE"]

                w_pv = [t_pt, prev.get(f"otfree_{bank_o.name}"),
                        ("GC", pr.cnt["GC"])]
                w_pv.append(("AC", ck_bank_tok) if h % 2 == 0 else ("AC", cv_bank_tok))
                vsrcs = [
                    (cv_st[0:128, 0, (hh + h) * 128:(hh + h + 1) * 128], 128, 0),
                    (cv_st[0:(W - 128), 1, (hh + h) * 128:(hh + h + 1) * 128], W - 128, 1),
                    (vring[:, (g - 2) % 4, h * 128:(h + 1) * 128], 128, 2),
                    (vring[:, (g - 1) % 4, h * 128:(h + 1) * 128], 128, 3),
                    (vring[:, g % 4, h * 128:(h + 1) * 128], 128, 4),
                ]
                for ci, (vs, kk, pc) in enumerate(vsrcs):
                    st("tensor", "matmul", bank_o[:, 0:128], vs, ptb[h % 2][0:kk, pc, :],
                       start=(ci == 0), stop=(ci == 4), ctr="TE",
                       waits=w_pv if ci == 0 else ())
                t_pv = pr.cnt["TE"]
                prev[f"ptb_used_{h % 2}"] = t_pv
                vread_done[(sw, g)] = t_pv
                w_oe = [("TE", t_pv)]
                if h == 0:
                    w_oe.append(("GS", oscr_tok.get((sw, i - 1))))
                t_oe = st("scalar", "activation", otile[:, h, :], bank_o[:, 0:128],
                          AF.Copy, ctr="AC", waits=w_oe)
                prev[f"otfree_{bank_o.name}"] = ("AC", t_oe)
                return t_oe

            for h in range(8):
                if h >= 3:
                    tp_pv(h - 3)
                scores_softmax(h)
                if h >= 1:
                    norm_p(h - 1)
            norm_p(7)
            tp_pv(5)
            tp_pv(6)
            last_oe = tp_pv(7)

            prev[f"mask_used_{p}"] = pr.cnt["DV"]
            oscr_tok[(sw, i)] = st(
                "gpsimd", "dma_start",
                d_oscr[i, hh:hh + 8].rearrange("c q t -> q c t"), otile[:],
                ctr="GS", amt=16, waits=[("AC", last_oe)])

        pend = None
        emit_xln(0)
        for g in range(NSW):
            own = g >= 16
            i = g - 16
            p = g % 2
            slot = g % 4
            t_rc = prev.get(f"rc_tok_{p}")
            t_rs = prev.get(f"rs_tok_{p}")
            t_m = None
            if own:
                W = 8 * g + 8
                t_m = st("gpsimd", "dma_start", mskb[p][:, 0:W + 384],
                         d_msk[i, :, 0:W + 384], ctr="GM", amt=16,
                         waits=[("DV", prev.get(f"mask_used_{p}"))])

            t_ln = ln_of.pop(g)

            emit_tgroup([(xnb[p][:, c * 128:(c + 1) * 128], 128) for c in range(8)],
                        xntb[:, 0:8, :], [t_ln])
            t_xnt = emit_tgroup(
                [(xnb[p][:, c * 128:(c + 1) * 128], 128) for c in range(8, 16)],
                xntb[:, 8:16, :], [t_ln])
            prev[f"xn_used_{p}"] = pr.cnt["TE"]

            def proj(wsb, out_aps, banks, extra=()):
                toks = []
                for cg in range(2):
                    bank = banks[cg]
                    w_mm = [t_xnt, ("GW", tok_w),
                            prev.get(f"bfree_{bank.name}")] + list(extra)
                    for dc in range(16):
                        st("tensor", "matmul", bank[:], xntb[:, dc, :],
                           wsb[:, dc, cg * 512:(cg + 1) * 512],
                           start=(dc == 0), stop=(dc == 15), ctr="TE",
                           waits=w_mm if dc == 0 else ())
                    tok = st("scalar", "activation", out_aps[cg], bank[:], AF.Copy,
                             ctr="AC", waits=[("TE", pr.cnt["TE"])])
                    prev[f"bfree_{bank.name}"] = ("AC", tok)
                    toks.append(tok)
                return toks[-1]

            t_q = None
            if own:
                t_q = proj(wqs, [qtok[p][:, 0:512], qtok[p][:, 512:1024]], PSQ)
            t_k = proj(wks, [ktok[p][:, 0:512], ktok[p][:, 512:1024]], PSS)
            t_v = proj(wvs, [vring[:, slot, 0:512], vring[:, slot, 512:1024]], PSQ,
                       extra=[("TE", vread_done.get((sw, g - 2)))])

            def rope(buf, tok_in):
                bv = buf[:].rearrange("q (h t) -> q h t", t=128)
                e, o = bv[:, :, 0:64], bv[:, :, 64:128]
                c = rcb[p][:].rearrange("q (h t) -> q h t", t=64)
                s_ = rsb[p][:].rearrange("q (h t) -> q h t", t=64)
                r1 = rs1[:].rearrange("q (h t) -> q h t", t=64)
                r2 = rs2[:].rearrange("q (h t) -> q h t", t=64)
                r3 = rs3[:].rearrange("q (h t) -> q h t", t=64)
                st("vector", "tensor_tensor", r1, e, c, op=ALU.mult, ctr="DV",
                   waits=[(f"SR{p}", t_rs), ("AC", tok_in)])
                st("vector", "tensor_tensor", r2, o, s_, op=ALU.mult, ctr="DV")
                st("vector", "tensor_tensor", r3, e, s_, op=ALU.mult, ctr="DV")
                st("vector", "tensor_tensor", e, r1, r2, op=ALU.subtract, ctr="DV")
                st("vector", "tensor_tensor", r1, o, c, op=ALU.mult, ctr="DV")
                return st("vector", "tensor_tensor", o, r3, r1, op=ALU.add, ctr="DV")

            t_ropek = rope(ktok[p], t_k)
            if own:
                t_ropeq = rope(qtok[p], t_q)
            prev[f"rope_done_{p}"] = pr.cnt["DV"]

            t_kfeat = prev.get("t_kfeat")
            if g >= 13:
                t_kfeat = emit_tgroup(
                    [(ktok[p][:, c * 128:(c + 1) * 128], 128) for c in range(8)],
                    kring[:, slot, :, :], [("DV", t_ropek)],
                    extra_ev=[("TE", scores_done.get((sw, g - 2)))])
                prev["t_kfeat"] = t_kfeat
            if own:
                t_qfeat = emit_tgroup(
                    [(qtok[p][:, c * 128:(c + 1) * 128], 128) for c in range(8)],
                    qft[p][:, :, :], [("DV", t_ropeq)],
                    extra_ev=[("TE", scores_done.get((sw, g - 2)))])

            w_pso = [("AC", prev.get("ck_evac")), prev.get("otfree_PSO"),
                     ("DV", t_ropek), ("GK", tok_consts)]
            for h in range(8):
                st("tensor", "matmul", PSO[:, 8 * h:8 * h + 8],
                   ktok[p][:, h * 128:(h + 1) * 128], p8sb[:],
                   start=(h == 0), stop=(h == 7), ctr="TE",
                   waits=w_pso if h == 0 else ())
            c0 = 8 * g
            t_ckev = st("scalar", "activation", ck_st[:, hh:hh + 8, c0:c0 + 8],
                        PSO[:, 0:64].rearrange("q (a b) -> q a b", b=8), AF.Copy,
                        ctr="AC", waits=[("TE", pr.cnt["TE"])])
            prev["ck_evac"] = t_ckev

            t1 = st("tensor", "matmul", PSX[0:8, 0:512], p8sb[:], vring[:, slot, 0:512],
                    start=True, stop=True, ctr="TE",
                    waits=[("AC", prev.get("cv_ev0")), prev.get("otfree_PSX"),
                           ("AC", t_v)])
            t2 = st("tensor", "matmul", PSS[0][0:8, 0:512], p8sb[:], vring[:, slot, 512:1024],
                    start=True, stop=True, ctr="TE",
                    waits=[("AC", prev.get("cv_ev1")), prev.get(f"bfree_{PSS[0].name}")])
            t_cv0 = st("scalar", "activation", cvstage[:, 0:512], PSX[0:8, 0:512], AF.Copy,
                       ctr="AC", waits=[("TE", t1), ("GC", prev.get("cv_moved"))])
            t_cv1 = st("scalar", "activation", cvstage[:, 512:1024], PSS[0][0:8, 0:512],
                       AF.Copy, ctr="AC", waits=[("TE", t2)])
            prev["cv_ev0"], prev["cv_ev1"] = t_cv0, t_cv1
            prev[f"bfree_{PSS[0].name}"] = ("AC", t_cv1)
            bt, rr = g // 16, 8 * (g % 16)
            t_cvmove = st("gpsimd", "dma_start",
                          cv_st[rr:rr + 8, bt, hh * 128:hh * 128 + 1024], cvstage[:],
                          ctr="GC", amt=16, waits=[("AC", t_cv1)])
            prev["cv_moved"] = t_cvmove

            if g + 1 < NSW:
                emit_xln(g + 1)

            if pend is not None:
                emit_attention(pend, t_ckev, t_cv0)
                pend = None
            if not own:
                scores_done[(sw, g)] = pr.cnt["TE"]
                vread_done[(sw, g)] = pr.cnt["TE"]
                continue
            pend = dict(g=g, i=i, p=p, W=W, t_m=t_m, t_qfeat=t_qfeat,
                        t_kfeat=t_kfeat, t_ckev=t_ckev, t_cvmove=t_cvmove)

        if pend is not None:
            emit_attention(pend, pend["t_ckev"], prev["cv_ev0"])
            pend = None

        last_qkv_te = pr.cnt["TE"]

    sweep_te, sweep_dv, sweep_ac = pr.cnt["TE"], pr.cnt["DV"], pr.cnt["AC"]
    sweep_dmas = [("SX0", pr.cnt["SX0"]), ("SX1", pr.cnt["SX1"]),
                  ("SR0", pr.cnt["SR0"]), ("SR1", pr.cnt["SR1"]),
                  ("GC", pr.cnt["GC"]), ("GS", pr.cnt["GS"])]
    if "dbg_ck" in dbg:
        st("sync", "dma_start", dbg["dbg_ck"], ck_st[:], ctr="ST", amt=16,
           waits=[("AC", sweep_ac)])
    if "dbg_cv" in dbg:
        st("sync", "dma_start", dbg["dbg_cv"], cv_st[:], ctr="ST", amt=16,
           waits=[("GC", pr.cnt["GC"])])

    w_bar = [("TE", sweep_te), ("DV", sweep_dv), ("AC", sweep_ac)] + sweep_dmas
    tok_wo = None
    for c in range(16):
        tok_wo = st("gpsimd", "dma_start", wos[:, c, :],
                    d_woT[c * 128:(c + 1) * 128, :], ctr="GW", amt=16,
                    waits=w_bar if c == 0 else ())

    prev3 = {}
    xa_spill = {}

    def emit_ln2(i):
        p = i % 2
        t_xa = prev3[f"xa_tok_{p}"]
        t_ln2 = emit_ln(xab[p], xn2b[p], sq3,
                        waits_dv=[("DV", t_xa), ("TE", prev3.get(f"xn2_used_{p}"))],
                        waits_ac=[("DV", t_xa), ("DV", prev3.get("ln2_out"))])
        prev3["ln2_out"] = pr.cnt["DV"]
        emit_tgroup([(xn2b[p][:, c * 128:(c + 1) * 128], 128) for c in range(8)],
                    xn2t[:, 0:8, i * 128:(i + 1) * 128], [t_ln2])
        emit_tgroup([(xn2b[p][:, c * 128:(c + 1) * 128], 128) for c in range(8, 16)],
                    xn2t[:, 8:16, i * 128:(i + 1) * 128], [t_ln2])
        prev3[f"xn2_used_{p}"] = pr.cnt["TE"]

    for i in range(NT):
        p = i % 2
        t_o = st("sync", "dma_start", o3b[p][:], d_oscr[i].rearrange("c q t -> q c t"),
                 ctr=f"S3A{p}", amt=16,
                 waits=[("GS", pr.cnt["GS"]),
                        ("TE", prev3.get(f"o_used_{p}")), ("TE", sweep_te)])
        t_x3 = st("sync", "dma_start", x3b[p][:], d_xo[i * 128:(i + 1) * 128, :],
                  ctr=f"S3B{p}", amt=16, waits=[("DV", prev3.get(f"x3_used_{p}"))])
        for q in range(4):
            bank = PSQ[q % 2]
            w_mm = [(f"S3A{p}", t_o), ("GW", tok_wo), ("DV", prev3.get(f"b3_{bank.name}"))]
            for c in range(16):
                st("tensor", "matmul", bank[:], o3b[p][:, c, :],
                   wos[:, c, q * 512:(q + 1) * 512], start=(c == 0), stop=(c == 15),
                   ctr="TE", waits=w_mm if c == 0 else ())
            w_ev = [("TE", pr.cnt["TE"]), (f"S3B{p}", t_x3)]
            if q == 0 and i >= 2:
                w_ev.append(("SSP", pr.cnt["SSP"]))
            tok = st("vector", "tensor_tensor", xab[p][:, q * 512:(q + 1) * 512],
                     bank[:], x3b[p][:, q * 512:(q + 1) * 512], op=ALU.add,
                     ctr="DV", waits=w_ev)
            prev3[f"b3_{bank.name}"] = tok
        prev3[f"o_used_{p}"] = pr.cnt["TE"]
        prev3[f"x3_used_{p}"] = pr.cnt["DV"]
        t_xa = prev3[f"b3_{PSQ[1].name}"]
        prev3[f"xa_tok_{p}"] = t_xa
        xa_spill[i] = st("sync", "dma_start", d_xascr[i * 128:(i + 1) * 128, :],
                         xab[p][:], ctr="SSP", amt=16, waits=[("DV", t_xa)])
        if i >= 1:
            emit_ln2(i - 1)
    emit_ln2(NT - 1)

    p3_te, p3_dv, p3_ac = pr.cnt["TE"], pr.cnt["DV"], pr.cnt["AC"]
    p3_dmas = [("S3A0", pr.cnt["S3A0"]), ("S3A1", pr.cnt["S3A1"]),
                ("S3B0", pr.cnt["S3B0"]), ("S3B1", pr.cnt["S3B1"]), ("SSP", pr.cnt["SSP"])]
    if "dbg_xa" in dbg:
        st("sync", "dma_start", dbg["dbg_xa"], d_xascr[0:128, :], ctr="ST", amt=16,
           waits=[("SSP", pr.cnt["SSP"])])

    bar4 = [("TE", p3_te), ("DV", p3_dv), ("AC", p3_ac)] + p3_dmas
    prev4 = {}
    for hc in range(64):
        bp = hc % 2
        t_w0 = st("gpsimd", "dma_start", wg0b[bp][:], d_wg0[hc], ctr=f"GA{bp}", amt=16,
                  waits=[("TE", prev4.get(f"wg_used_{bp}"))] + (bar4 if hc == 0 else []))
        t_w1 = st("gpsimd", "dma_start", wg1b[bp][:], d_wg1[hc], ctr=f"GA{bp}", amt=16)
        for ts in range(4):
            b0 = PSQ[0] if ts % 2 == 0 else PSS[0]
            b1 = PSQ[1] if ts % 2 == 0 else PSS[1]
            gp_ = ts % 2
            w_g0 = [(f"GA{bp}", t_w1), ("AC", prev4.get(f"gel_{b0.name}")),
                    ("DV", prev4.get(f"mul_{b1.name}"))]
            if hc == 0 and ts == 0:
                w_g0 = w_g0 + bar4
            for dc in range(8):
                st("tensor", "matmul", b0[:], wg0b[bp][:, 2 * dc:2 * dc + 2, :],
                   xn2t[:, 2 * dc:2 * dc + 2, ts * 512:(ts + 1) * 512],
                   start=(dc == 0), stop=(dc == 7), ctr="TE",
                   perf_mode=mybir.MatmulPerfMode.DoubleRow,
                   waits=w_g0 if dc == 0 else ())
            t_g0 = pr.cnt["TE"]
            for dc in range(8):
                st("tensor", "matmul", b1[:], wg1b[bp][:, 2 * dc:2 * dc + 2, :],
                   xn2t[:, 2 * dc:2 * dc + 2, ts * 512:(ts + 1) * 512],
                   start=(dc == 0), stop=(dc == 7), ctr="TE",
                   perf_mode=mybir.MatmulPerfMode.DoubleRow)
            t_g1 = pr.cnt["TE"]
            t_gel = st("scalar", "activation", gelb[gp_][:], b0[:], AF.Gelu_apprx_tanh,
                       ctr="AC", waits=[("TE", t_g0), ("DV", prev4.get(f"gelu_{gp_}"))])
            prev4[f"gel_{b0.name}"] = t_gel
            t_mul = st("vector", "tensor_tensor", act[:, hc, ts * 512:(ts + 1) * 512],
                       gelb[gp_][:], b1[:], op=ALU.mult, ctr="DV",
                       waits=[("TE", t_g1), ("AC", t_gel)])
            prev4[f"mul_{b1.name}"] = t_mul
            prev4[f"gelu_{gp_}"] = t_mul
        prev4[f"wg_used_{bp}"] = pr.cnt["TE"]

    p4_te, p4_dv, p4_ac = pr.cnt["TE"], pr.cnt["DV"], pr.cnt["AC"]
    if "dbg_act" in dbg:
        st("sync", "dma_start", dbg["dbg_act"], act[:], ctr="ST", amt=16,
           waits=[("DV", p4_dv)])
    if "dbg_xn2t" in dbg:
        st("sync", "dma_start", dbg["dbg_xn2t"], xn2t[:], ctr="ST", amt=16,
           waits=[("DV", p3_dv)])

    prev5 = {}
    d_wl_v = d_wl.rearrange("(c p) m -> p c m", p=128)
    for q in range(8):
        bi_ = q % 2
        buf = wlb[bi_]
        if q == 0:
            w_wl = [("TE", sweep_te)]
        elif q == 1:
            w_wl = [("TE", p4_te)]
        else:
            w_wl = [("TE", prev5.get(f"wl_used_{bi_}"))]
        tok_wl = None
        for hg in range(8):
            tok_wl = st("gpsimd", "dma_start", buf[:, hg * 8:(hg + 1) * 8, :],
                        d_wl_v[:, hg * 8:(hg + 1) * 8, q * 256:(q + 1) * 256],
                        ctr=f"WL{bi_}", amt=16, waits=w_wl if hg == 0 else ())
        for i in range(NT):
            p = i % 2
            bank = PSQ[i % 2]
            t_xa5 = st("sync", "dma_start", xa5b[p][:],
                       d_xascr[i * 128:(i + 1) * 128, q * 256:(q + 1) * 256],
                       ctr="SA5", amt=16,
                       waits=[("DV", prev5.get(f"xa5_used_{p}")), ("SSP", pr.cnt["SSP"]),
                              ("AC", p4_ac)])
            w_mm = [(f"WL{bi_}", tok_wl), ("DV", prev5.get(f"b5_{bank.name}")), ("DV", p4_dv)]
            for hc in range(32):
                st("tensor", "matmul", bank[:, 0:256],
                   act[:, 2 * hc:2 * hc + 2, i * 128:(i + 1) * 128],
                   buf[:, 2 * hc:2 * hc + 2, :],
                   start=(hc == 0), stop=(hc == 31), ctr="TE",
                   perf_mode=mybir.MatmulPerfMode.DoubleRow,
                   waits=w_mm if hc == 0 else ())
            t_dn = pr.cnt["TE"]
            op_ = i % 2
            t_ev = st("vector", "tensor_tensor", outst[op_][:], bank[:, 0:256],
                      xa5b[p][:], op=ALU.add, ctr="DV",
                      waits=[("TE", t_dn), ("SA5", t_xa5),
                             ("ST", pr.cnt["ST"])])
            prev5[f"b5_{bank.name}"] = t_ev
            prev5[f"xa5_used_{p}"] = t_ev
            tok_out = st("sync", "dma_start",
                         d_y[i * 128:(i + 1) * 128, q * 256:(q + 1) * 256],
                         outst[op_][:], ctr="ST", amt=16, waits=[("DV", t_ev)])
            prev5[f"odma_{op_}"] = tok_out
        prev5[f"wl_used_{bi_}"] = pr.cnt["TE"]

    st("sync", None, waits=[("ST", pr.cnt["ST"])])

    with nc.Block() as block:
        pr.run(block)
    es.close()
    return nc


_CACHE = {}


def _host_inputs(x, wq, wk, wv, wo, wg, wl):
    x = np.ascontiguousarray(np.asarray(x, dtype=np.float32))
    wq = np.asarray(wq, dtype=np.float32)
    wk = np.asarray(wk, dtype=np.float32)
    wv = np.asarray(wv, dtype=np.float32)
    wo = np.asarray(wo, dtype=np.float32)
    wg = np.asarray(wg, dtype=np.float32)
    wl = np.ascontiguousarray(np.asarray(wl, dtype=np.float32))

    perm = np.concatenate([np.concatenate([h * 128 + np.arange(0, 128, 2),
                                           h * 128 + np.arange(1, 128, 2)])
                           for h in range(16)])
    wqT = np.ascontiguousarray(wq.T[:, perm])
    wkT = np.ascontiguousarray(wk.T[:, perm])
    wvT = np.ascontiguousarray(wv.T)
    woT = np.ascontiguousarray(wo.T)
    wg0t = np.ascontiguousarray(wg[0].reshape(16, 128, 64, 128).transpose(2, 1, 0, 3))
    wg1t = np.ascontiguousarray(wg[1].reshape(16, 128, 64, 128).transpose(2, 1, 0, 3))

    inv = 1.0 / 10000.0 ** (np.arange(0, DH, 2, dtype=np.float32) / DH)
    f = np.outer(np.arange(S, dtype=np.float32), inv)
    fc = np.exp(1j * f).astype(np.complex64)
    cosg = np.tile(np.real(fc).astype(np.float32), (1, 8))
    sing = np.tile(np.imag(fc).astype(np.float32), (1, 8))

    p8c = np.zeros((128, 8), dtype=np.float32)
    for t in range(128):
        p8c[t, t // 16] = 1.0 / 16.0
    identc = np.eye(128, dtype=np.float32)

    cores = []
    for bi in range(B):
        for half in range(2):
            t0 = 2048 * half
            xp = np.zeros((2048, D), dtype=np.float32) if half == 0 \
                else np.ascontiguousarray(x[bi, 0:2048])
            xo = np.ascontiguousarray(x[bi, t0:t0 + 2048])
            rc = np.zeros((4096, 512), dtype=np.float32)
            rs = np.zeros((4096, 512), dtype=np.float32)
            if half == 0:
                rc[2048:], rs[2048:] = cosg[0:2048], sing[0:2048]
            else:
                rc[:], rs[:] = cosg, sing
            masks = np.full((NT, 128, 640), NEG, dtype=np.float32)
            qi = np.arange(128)[:, None]
            m = np.arange(128)[None, :]
            for i in range(NT):
                T_abs = t0 + 128 * i
                W = 8 * (16 + i) + 8
                jj = np.arange(W)[None, :]
                blk_abs = t0 // 16 - 128 + jj
                ok = (blk_abs >= 0) & (blk_abs < (T_abs + qi + 1) // 16)
                masks[i, :, 0:W] = np.where(ok, 0.0, NEG)
                k0 = ((T_abs - 256 + m) >= 0) & (qi < m)
                masks[i, :, W:W + 128] = np.where(k0, 0.0, NEG)
                k1 = np.broadcast_to((T_abs - 128 + m) >= 0, (128, 128))
                masks[i, :, W + 128:W + 256] = np.where(k1, 0.0, NEG)
                masks[i, :, W + 256:W + 384] = np.where(m <= qi, 0.0, NEG)
            cores.append(dict(xp=xp, xo=xo, wqT=wqT, wkT=wkT, wvT=wvT, woT=woT,
                              wg0t=wg0t, wg1t=wg1t, wl=wl, ropec=rc, ropes=rs,
                              masks=masks, p8c=p8c, identc=identc))
    return cores


def kernel(L0_attn_x, L0_attn_wq, L0_attn_wk, L0_attn_wv, L0_attn_wo,
           L0_ffn_w_gating, L0_ffn_w_linear):
    cores = _host_inputs(L0_attn_x, L0_attn_wq, L0_attn_wk, L0_attn_wv,
                         L0_attn_wo, L0_ffn_w_gating, L0_ffn_w_linear)
    if "nc" not in _CACHE:
        _CACHE["nc"] = build_program()
    res = run_bass_kernel_spmd(_CACHE["nc"], cores, core_ids=list(range(8)))
    out = np.empty((B, S, D), dtype=np.float32)
    for bi in range(B):
        for half in range(2):
            out[bi, 2048 * half:2048 * (half + 1)] = res.results[bi * 2 + half]["y"]
    return out
